# version 04 (final)
# baseline (speedup 1.0000x reference)
import numpy as np
import concourse.bass as bass
import concourse.mybir as mybir
from concourse.bass_utils import run_bass_kernel_spmd
from concourse import library_config

F32 = mybir.dt.float32
BF16 = mybir.dt.bfloat16
I16 = mybir.dt.int16
I32 = mybir.dt.int32
U32 = mybir.dt.uint32
U8 = mybir.dt.uint8
AF = mybir.ActivationFunctionType
ALU = mybir.AluOpType
AX = mybir.AxisListType

NCORE = 8
B, T, D, H, DH = 2, 16384, 1024, 16, 64
TOK = B * T
TPC = TOK // NCORE


class Prog:

    def __init__(self, nc):
        self.nc = nc
        self.ops = []
        self.counts = {"g": 0, "gdma": 0}

    def op(self, eng, fn, inc=1, sem="g"):
        self.ops.append((eng, fn, inc, sem, dict(self.counts)))
        self.counts[sem] += inc

    def dma(self, src_eng, out, in_, **kw):
        self.op(src_eng, lambda e: e.dma_start(out=out, in_=in_, **kw), inc=16)

    def gdma(self, out, in_, **kw):
        self.op("gpsimd", lambda e: e.dma_start(out=out, in_=in_, **kw), inc=16, sem="gdma")

    def v(self, fn):
        self.op("vector", fn)

    def s(self, fn):
        self.op("scalar", fn)

    def t(self, fn):
        self.op("tensor", fn)

    def g(self, fn):
        self.op("gpsimd", fn)

    def mm_group(self, emits):
        def fn(eng):
            last = None
            for e in emits:
                last = e(eng)
            return last
        self.op("tensor", fn)

    def finalize(self, blk, sems):
        names = ["sync", "vector", "scalar", "tensor", "gpsimd"]
        for name in names:
            def mk(name):
                def inner(eng):
                    for (engname, fn, inc, semk, pre) in self.ops:
                        if engname != name:
                            continue
                        for k, v in pre.items():
                            if v > 0:
                                eng.wait_ge(sems[k], v)
                        r = fn(eng)
                        r.then_inc(sems[semk], inc)
                return inner
            getattr(blk, name)(mk(name))
        def fin(eng):
            for k in self.counts:
                if self.counts[k] > 0:
                    eng.wait_ge(sems[k], self.counts[k])
        blk.sync(fin)


def _ap(x):
    return x if isinstance(x, bass.AP) else x[:]


def ln_tile(p, out, xin, tmp, mu, var, epsap, n=D):
    out, xin, tmp, mu, var = map(_ap, (out, xin, tmp, mu, var))
    p.v(lambda e: e.reduce_sum(out=mu, in_=xin, axis=AX.X))
    p.s(lambda e: e.mul(out=mu, in_=mu, mul=1.0 / n))
    p.v(lambda e: e.tensor_scalar(out=out, in0=xin, scalar1=mu, scalar2=None, op0=ALU.subtract))
    p.s(lambda e: e.activation(out=tmp, in_=out, func=AF.Square))
    p.v(lambda e: e.reduce_sum(out=var, in_=tmp, axis=AX.X))
    p.s(lambda e: e.mul(out=var, in_=var, mul=1.0 / n))
    p.s(lambda e: e.activation(out=var, in_=var, func=AF.Ln, bias=epsap))
    p.s(lambda e: e.activation(out=var, in_=var, func=AF.Exp, scale=-0.5))
    p.v(lambda e: e.tensor_scalar(out=out, in0=out, scalar1=var, scalar2=None, op0=ALU.mult))


def build_B():
    nc = bass.Bass("TRN2", target_bir_lowering=False)
    xs = nc.dram_tensor("xs", [TPC + 1, D], F32, kind="ExternalInput").ap()
    wT = {k: nc.dram_tensor("w" + k + "T", [D, D], F32, kind="ExternalInput").ap() for k in "rkvw"}
    tm = nc.dram_tensor("tm", [4, D], F32, kind="ExternalInput").ap()
    ones_in = nc.dram_tensor("ones_in", [1, 128], F32, kind="ExternalInput").ap()
    idm_in = nc.dram_tensor("idm", [128, 128], F32, kind="ExternalInput").ap()
    r_t = nc.dram_tensor("r_t", [TPC, D], F32, kind="ExternalOutput").ap()
    v_t = nc.dram_tensor("v_t", [TPC, D], F32, kind="ExternalOutput").ap()
    outT = {k: nc.dram_tensor(k + "T_o", [D, TPC], F32, kind="ExternalOutput").ap() for k in ["r", "k", "w", "xn"]}

    NT = TPC // 128
    import contextlib
    st = contextlib.ExitStack()
    E = st.enter_context
    wb = E(nc.sbuf_tensor("wb", [128, 8 * D * 4], BF16))
    tmb = E(nc.sbuf_tensor("tmb", [128, 4 * D], F32))
    ones = E(nc.sbuf_tensor("ones", [1, 128], F32))
    idm = E(nc.sbuf_tensor("idms", [128, 128], F32))
    xt = E(nc.sbuf_tensor("xt", [128, D], F32))
    xp = E(nc.sbuf_tensor("xp", [128, D], F32))
    xn = E(nc.sbuf_tensor("xn", [128, D], F32))
    prevrow = E(nc.sbuf_tensor("prevrow", [1, D], F32))
    prevn = E(nc.sbuf_tensor("prevn", [1, D], F32))
    delta = E(nc.sbuf_tensor("delta", [128, D], F32))
    xm = E(nc.sbuf_tensor("xm", [128, D], F32))
    xmTb = E(nc.sbuf_tensor("xmTb", [128, D], BF16))
    projC = E(nc.sbuf_tensor("projC", [128, 8 * 128], F32))
    projT = E(nc.sbuf_tensor("projT", [128, D], F32))
    tmp = E(nc.sbuf_tensor("tmp", [128, D], F32))
    tmp2 = E(nc.sbuf_tensor("tmp2", [128, D], F32))
    mu = E(nc.sbuf_tensor("mu", [128, 1], F32))
    var = E(nc.sbuf_tensor("var", [128, 1], F32))
    ps = E(nc.psum_tensor("ps", [128, 512], F32))
    pst = E(nc.psum_tensor("pst", [128, 128], F32))
    sg = E(nc.semaphore("sg"))
    sd = E(nc.semaphore("sd"))
    blk = E(nc.Block())
    cb = E(nc.sbuf_tensor("cb", [128, 2], F32))
    if True:
        p = Prog(nc)
        p.g(lambda e: e.memset(cb[:, 0:1], 1e-5))
        p.g(lambda e: e.memset(cb[:, 1:2], 1.0))
        p.dma("sync", ones[:], ones_in[:])
        p.dma("sync", idm[:], idm_in[:])
        for i, k in enumerate("rkvw"):
            for c in range(8):
                p.gdma(wb[:, (i * 8 + c) * D:(i * 8 + c + 1) * D], wT[k][c * 128:(c + 1) * 128, :])
        for i in range(4):
            p.dma("sync", tmb[:, i * D:(i + 1) * D], tm[i:i + 1, :].broadcast_to((128, D)))

        for it in range(NT):
            t0 = it * 128
            p.dma("sync", xt[:], xs[1 + t0:1 + t0 + 128, :])
            p.dma("sync", prevrow[:], xs[t0:t0 + 1, :])
            ln_tile(p, xn, xt, tmp, mu, var, cb[:, 0:1])
            ln_tile(p, prevn, prevrow, tmp[0:1, :], mu[0:1, :], var[0:1, :], cb[0:1, 0:1])
            p.dma("sync", xp[1:128, :], xn[0:127, :])
            p.dma("sync", xp[0:1, :], prevn[:])
            for c in range(8):
                p.t(lambda e, c=c: e.transpose(out=pst[:], in_=xn[:, c * 128:(c + 1) * 128], identity=idm[:]))
                p.v(lambda e, c=c: e.tensor_copy(out=tmp[:, c * 128:(c + 1) * 128], in_=pst[:]))
            p.dma("sync", outT["xn"][:, t0:t0 + 128].rearrange("(c p) t -> p c t", p=128), tmp[:])
            p.v(lambda e: e.tensor_tensor(out=delta[:], in0=xn[:], in1=xp[:], op=ALU.subtract))
            for i, k in enumerate("rkvw"):
                p.v(lambda e, i=i: e.tensor_tensor(out=xm[:], in0=delta[:], in1=tmb[:, i * D:(i + 1) * D], op=ALU.mult))
                p.v(lambda e: e.tensor_tensor(out=xm[:], in0=xm[:], in1=xp[:], op=ALU.add))
                for c in range(8):
                    p.t(lambda e, c=c: e.transpose(out=pst[:], in_=xm[:, c * 128:(c + 1) * 128], identity=idm[:]))
                    p.v(lambda e, c=c: e.tensor_copy(out=xmTb[:, c * 128:(c + 1) * 128], in_=pst[:]))
                for oc in range(8):
                    def mmrun(e, i=i, oc=oc):
                        r = None
                        for c in range(8):
                            r = e.matmul(out=ps[:, (oc % 4) * 128:(oc % 4) * 128 + 128],
                                         lhsT=wb[:, (i * 8 + c) * D + oc * 128:(i * 8 + c) * D + oc * 128 + 128],
                                         rhs=xmTb[:, c * 128:(c + 1) * 128],
                                         start=(c == 0), stop=(c == 7))
                        return r
                    p.op("tensor", mmrun)
                    if k == "r":
                        p.s(lambda e, oc=oc: e.activation(out=projC[:, oc * 128:(oc + 1) * 128],
                                                          in_=ps[:, (oc % 4) * 128:(oc % 4) * 128 + 128], func=AF.Sigmoid))
                    elif k == "w":
                        po = ps[:, (oc % 4) * 128:(oc % 4) * 128 + 128]
                        sl = projC[:, oc * 128:(oc + 1) * 128]
                        p.s(lambda e, po=po: e.activation(out=tmp[:, 0:128], in_=po, func=AF.Abs))
                        p.s(lambda e: e.activation(out=tmp[:, 0:128], in_=tmp[:, 0:128], func=AF.Exp, scale=-1.0))
                        p.s(lambda e: e.activation(out=tmp[:, 0:128], in_=tmp[:, 0:128], func=AF.Ln, bias=cb[:, 1:2]))
                        p.s(lambda e, po=po: e.activation(out=tmp[:, 128:256], in_=po, func=AF.Relu, scale=-1.0))
                        p.v(lambda e: e.tensor_tensor(out=tmp[:, 0:128], in0=tmp[:, 0:128], in1=tmp[:, 128:256], op=ALU.add))
                        p.s(lambda e, sl=sl: e.activation(out=sl, in_=tmp[:, 0:128], func=AF.Copy, scale=-1.0, bias=-1e-4))
                    else:
                        p.s(lambda e, oc=oc: e.activation(out=projC[:, oc * 128:(oc + 1) * 128],
                                                          in_=ps[:, (oc % 4) * 128:(oc % 4) * 128 + 128], func=AF.Copy))
                if k in ("r", "k", "w"):
                    p.dma("sync", outT[k][:, t0:t0 + 128].rearrange("(c p) t -> p c t", p=128), projC[:])
                if k in ("r", "v"):
                    for c in range(8):
                        p.t(lambda e, c=c: e.transpose(out=pst[:], in_=projC[:, c * 128:(c + 1) * 128], identity=idm[:]))
                        p.v(lambda e, c=c: e.tensor_copy(out=projT[:, c * 128:(c + 1) * 128], in_=pst[:]))
                    dst = r_t if k == "r" else v_t
                    p.dma("sync", dst[t0:t0 + 128, :], projT[:])
        p.finalize(blk, {"g": sg, "gdma": sd})
    st.close()
    return nc


def host_B(inputs):
    x = np.asarray(inputs["L0_attn_x"], dtype=np.float32)
    xf = x.reshape(TOK, D)
    ones = np.ones((1, 128), np.float32)
    idm = np.eye(128, dtype=np.float32)
    tmm = np.stack([np.asarray(inputs["L0_attn_tm_" + k], np.float32) for k in "rkvw"])
    wTs = {k: np.ascontiguousarray(np.asarray(inputs["L0_attn_W_" + k], np.float32).T) for k in "rkvw"}
    maps = []
    for c in range(NCORE):
        lo = c * TPC
        xs = np.zeros((TPC + 1, D), np.float32)
        xs[1:] = xf[lo:lo + TPC]
        if lo % T != 0:
            xs[0] = xf[lo - 1]
        m = {"xs": xs, "tm": tmm, "ones_in": ones, "idm": idm}
        for k in "rkvw":
            m["w" + k + "T"] = wTs[k]
        maps.append(m)
    return maps


def build_C():
    nc = bass.Bass("TRN2", target_bir_lowering=False)
    CC = 128
    NCH = T // CC
    rCi = nc.dram_tensor("rC", [4, 64, T], F32, kind="ExternalInput").ap()
    kCi = nc.dram_tensor("kC", [4, 64, T], F32, kind="ExternalInput").ap()
    wCi = nc.dram_tensor("wC", [4, 64, T], F32, kind="ExternalInput").ap()
    rTi = nc.dram_tensor("rT", [4, T, 64], F32, kind="ExternalInput").ap()
    vTi = nc.dram_tensor("vT", [4, T, 64], F32, kind="ExternalInput").ap()
    ucoli = nc.dram_tensor("ucol", [256, 1], F32, kind="ExternalInput").ap()
    lnwi = nc.dram_tensor("lnw", [1, 64], F32, kind="ExternalInput").ap()
    lnbi = nc.dram_tensor("lnb", [1, 64], F32, kind="ExternalInput").ap()
    cmaski = nc.dram_tensor("cmaskT", [128, 128], U8, kind="ExternalInput").ap()
    idmbi = nc.dram_tensor("idmb", [128, 128], BF16, kind="ExternalInput").ap()
    o_t = nc.dram_tensor("o_t", [4, T, 64], F32, kind="ExternalOutput").ap()

    import contextlib
    st = contextlib.ExitStack()
    E = st.enter_context
    rC2 = E(nc.sbuf_tensor("rC2", [128, CC], F32))
    kC2 = E(nc.sbuf_tensor("kC2", [128, CC], F32))
    wC2 = E(nc.sbuf_tensor("wC2", [128, CC], F32))
    vTb = E(nc.sbuf_tensor("vTb", [128, 2 * 64], BF16))
    rTt = E(nc.sbuf_tensor("rTt", [128, 2 * 64], F32))
    cw = E(nc.sbuf_tensor("cw", [128, CC], F32))
    tmid = E(nc.sbuf_tensor("tmid", [128, CC], F32))
    shtmp = E(nc.sbuf_tensor("shtmp", [128, CC], F32))
    eprev = E(nc.sbuf_tensor("eprev", [128, CC], F32))
    eneg = E(nc.sbuf_tensor("eneg", [128, CC], F32))
    Rb = E(nc.sbuf_tensor("Rb", [128, CC], BF16))
    Kb = E(nc.sbuf_tensor("Kb", [128, CC], BF16))
    K2b = E(nc.sbuf_tensor("K2b", [128, CC], BF16))
    K2Tb = E(nc.sbuf_tensor("K2Tb", [128, 64], BF16))
    euk = E(nc.sbuf_tensor("euk", [128, CC], F32))
    rk = E(nc.sbuf_tensor("rk", [128, CC], F32))
    Ssb = E(nc.sbuf_tensor("Ssb", [128, 64], F32))
    Sdb = E(nc.sbuf_tensor("Sdb", [128, 64], BF16))
    Mmb = E(nc.sbuf_tensor("Mmb", [128, 128], BF16))
    obuf = E(nc.sbuf_tensor("obuf", [128, 64], F32))
    bon = E(nc.sbuf_tensor("bon", [128, 64], F32))
    zeros128 = E(nc.sbuf_tensor("zeros128", [128, 128], F32))
    cmask = E(nc.sbuf_tensor("cmask", [128, 128], U8))
    idmb = E(nc.sbuf_tensor("idmbs", [128, 128], BF16))
    ucol = E(nc.sbuf_tensor("ucols", [128, 2], F32))
    lnwB = E(nc.sbuf_tensor("lnwB", [128, 64], F32))
    lnbB = E(nc.sbuf_tensor("lnbB", [128, 64], F32))
    onesc = E(nc.sbuf_tensor("onesc", [128, 1], F32))
    ecol = E(nc.sbuf_tensor("ecol", [128, 4], F32))
    mu = E(nc.sbuf_tensor("mu", [128, 1], F32))
    var = E(nc.sbuf_tensor("var", [128, 1], F32))
    cb = E(nc.sbuf_tensor("cb", [128, 1], F32))
    gtmp = E(nc.sbuf_tensor("gtmp", [128, 64], F32))
    psM = E(nc.psum_tensor("psM", [128, 128], F32))
    psO = E(nc.psum_tensor("psO", [128, 64], F32))
    psD = E(nc.psum_tensor("psD", [128, 2], F32))
    psS = E(nc.psum_tensor("psS", [64, 64], F32))
    psK = E(nc.psum_tensor("psK", [128, 64], BF16))
    sg = E(nc.semaphore("sg"))
    sd = E(nc.semaphore("sd"))
    blk = E(nc.Block())

    p = Prog(nc)
    p.dma("sync", cmask[:], cmaski[:])
    p.dma("sync", idmb[:], idmbi[:])
    p.dma("sync", lnwB[:], lnwi[0:1, :].broadcast_to((128, 64)))
    p.dma("sync", lnbB[:], lnbi[0:1, :].broadcast_to((128, 64)))
    p.dma("sync", ucol[:, 0:1], ucoli[0:128, :])
    p.dma("sync", ucol[:, 1:2], ucoli[128:256, :])
    p.g(lambda e: e.memset(zeros128[:], 0.0))
    p.g(lambda e: e.memset(onesc[:], 1.0))
    p.g(lambda e: e.memset(cb[:], 1e-5))

    for g in range(2):
        hA, hB2 = 2 * g, 2 * g + 1
        p.g(lambda e: e.memset(Ssb[:], 0.0))
        for ch in range(NCH):
            c0 = ch * CC
            p.dma("sync", rC2[:], rCi[hA:hB2 + 1, :, c0:c0 + CC])
            p.dma("sync", kC2[:], kCi[hA:hB2 + 1, :, c0:c0 + CC])
            p.dma("sync", wC2[:], wCi[hA:hB2 + 1, :, c0:c0 + CC])
            p.gdma(vTb[:, 0:64], vTi[hA, c0:c0 + CC, :])
            p.gdma(vTb[:, 64:128], vTi[hB2, c0:c0 + CC, :])
            p.dma("sync", rTt[:, 0:64], rTi[hA, c0:c0 + CC, :])
            p.dma("sync", rTt[:, 64:128], rTi[hB2, c0:c0 + CC, :])
            p.v(lambda e: e.tensor_tensor_scan(out=cw[:], data0=wC2[:], data1=zeros128[:], initial=0.0, op0=ALU.add, op1=ALU.add))
            p.v(lambda e: e.tensor_scalar(out=tmid[:], in0=cw[:], scalar1=cw[:, 63:64], scalar2=None, op0=ALU.subtract))
            p.v(lambda e: e.tensor_copy(out=shtmp[:, 1:CC], in_=tmid[:, 0:CC - 1]))
            p.s(lambda e: e.activation(out=shtmp[:, 0:1], in_=cw[:, 63:64], func=AF.Copy, scale=-1.0))
            p.s(lambda e: e.activation(out=eprev[:], in_=shtmp[:], func=AF.Exp))
            p.v(lambda e: e.tensor_tensor(out=Rb[:], in0=rC2[:], in1=eprev[:], op=ALU.mult))
            p.s(lambda e: e.activation(out=eneg[:], in_=tmid[:], func=AF.Exp, scale=-1.0))
            p.v(lambda e: e.tensor_tensor(out=Kb[:], in0=kC2[:], in1=eneg[:], op=ALU.mult))
            p.v(lambda e: e.tensor_tensor(out=ecol[:, 0:1], in0=cw[:, CC - 1:CC], in1=cw[:, 63:64], op=ALU.subtract))
            p.s(lambda e: e.activation(out=ecol[:, 1:2], in_=ecol[:, 0:1], func=AF.Exp))
            p.v(lambda e: e.tensor_scalar(out=K2b[:], in0=Kb[:], scalar1=ecol[:, 1:2], scalar2=None, op0=ALU.mult))
            p.s(lambda e: e.activation(out=ecol[:, 2:3], in_=cw[:, 63:64], func=AF.Exp))
            p.s(lambda e: e.activation(out=ecol[:, 3:4], in_=cw[:, CC - 1:CC], func=AF.Exp))
            p.v(lambda e: e.tensor_scalar(out=Sdb[:], in0=Ssb[:], scalar1=ecol[:, 2:3], scalar2=None, op0=ALU.mult))
            p.s(lambda e: e.activation(out=euk[:], in_=kC2[:], func=AF.Exp, bias=ucol[:, g:g + 1]))
            p.v(lambda e: e.tensor_tensor(out=rk[:], in0=rC2[:], in1=euk[:], op=ALU.mult))
            def dmm(e):
                e.matmul(out=psD[:, 0:1], lhsT=rk[0:64, :], rhs=onesc[0:64, :], start=True, stop=False)
                return e.matmul(out=psD[:, 1:2], lhsT=rk[64:128, :], rhs=onesc[64:128, :], start=False, stop=True)
            p.op("tensor", dmm)
            for hh in range(2):
                pb = 64 * hh
                p.t(lambda e, pb=pb: e.matmul(out=psM[:], lhsT=Kb[pb:pb + 64, :], rhs=Rb[pb:pb + 64, :], start=True, stop=True))
                p.v(lambda e: e.select(out=Mmb[:], mask=cmask[:], on_true=psM[:], on_false=zeros128[:], add_drain=True))
                def omm(e, pb=pb):
                    e.matmul(out=psO[:], lhsT=Mmb[:], rhs=vTb[:, pb:pb + 64], start=True, stop=False)
                    return e.matmul(out=psO[:], lhsT=Rb[pb:pb + 64, :], rhs=Sdb[pb:pb + 64, :], start=False, stop=True)
                p.op("tensor", omm)
                p.v(lambda e, hh=hh, pb=pb: e.tensor_scalar(out=bon[:], in0=vTb[:, pb:pb + 64], scalar1=psD[:, hh:hh + 1], scalar2=None, op0=ALU.mult))
                p.v(lambda e: e.tensor_tensor(out=obuf[:], in0=psO[:], in1=bon[:], op=ALU.add))
                p.v(lambda e: e.reduce_sum(out=mu[:], in_=obuf[:], axis=AX.X))
                p.s(lambda e: e.mul(out=mu[:], in_=mu[:], mul=1.0 / 64))
                p.v(lambda e: e.tensor_scalar(out=obuf[:], in0=obuf[:], scalar1=mu[:], scalar2=None, op0=ALU.subtract))
                p.s(lambda e: e.activation(out=gtmp[:], in_=obuf[:], func=AF.Square))
                p.v(lambda e: e.reduce_sum(out=var[:], in_=gtmp[:], axis=AX.X))
                p.s(lambda e: e.mul(out=var[:], in_=var[:], mul=1.0 / 64))
                p.s(lambda e: e.activation(out=var[:], in_=var[:], func=AF.Ln, bias=cb[:]))
                p.s(lambda e: e.activation(out=var[:], in_=var[:], func=AF.Exp, scale=-0.5))
                p.v(lambda e: e.tensor_scalar(out=obuf[:], in0=obuf[:], scalar1=var[:], scalar2=None, op0=ALU.mult))
                p.v(lambda e: e.tensor_tensor(out=obuf[:], in0=obuf[:], in1=lnwB[:], op=ALU.mult))
                p.v(lambda e: e.tensor_tensor(out=obuf[:], in0=obuf[:], in1=lnbB[:], op=ALU.add))
                p.v(lambda e, pb=pb: e.tensor_tensor(out=obuf[:], in0=obuf[:], in1=rTt[:, pb:pb + 64], op=ALU.mult))
                p.dma("sync", o_t[hA + hh, c0:c0 + CC, :], obuf[:])
                p.t(lambda e, pb=pb: e.transpose(out=psK[:, 0:64], in_=K2b[pb:pb + 64, :], identity=idmb[pb:pb + 64, pb:pb + 64]))
                p.v(lambda e: e.tensor_copy(out=K2Tb[:], in_=psK[:, 0:64]))
                p.t(lambda e, pb=pb: e.matmul(out=psS[:], lhsT=K2Tb[:], rhs=vTb[:, pb:pb + 64], start=True, stop=True))
                p.v(lambda e, pb=pb: e.tensor_scalar(out=Ssb[pb:pb + 64, :], in0=Ssb[pb:pb + 64, :], scalar1=ecol[pb:pb + 64, 3:4], scalar2=None, op0=ALU.mult))
                p.v(lambda e, pb=pb: e.tensor_tensor(out=Ssb[pb:pb + 64, :], in0=Ssb[pb:pb + 64, :], in1=psS[:], op=ALU.add))
    p.finalize(blk, {"g": sg, "gdma": sd})
    st.close()
    return nc


def host_C(inputs, bres):
    import ml_dtypes
    cmaskT = (np.arange(128)[:, None] < np.arange(128)[None, :]).astype(np.uint8)
    idmb = np.eye(128).astype(ml_dtypes.bfloat16)
    lnw = np.asarray(inputs["L0_attn_ln_w"], np.float32).reshape(1, 64)
    lnb = np.asarray(inputs["L0_attn_ln_b"], np.float32).reshape(1, 64)
    u = np.asarray(inputs["L0_attn_u"], np.float32)
    maps = []
    for c in range(NCORE):
        bhs = [((c * 4 + k) // H, (c * 4 + k) % H) for k in range(4)]
        m = {
            "rC": np.stack([bres["rT"][b, h * 64:(h + 1) * 64, :] for b, h in bhs]),
            "kC": np.stack([bres["kT"][b, h * 64:(h + 1) * 64, :] for b, h in bhs]),
            "wC": np.stack([bres["wT"][b, h * 64:(h + 1) * 64, :] for b, h in bhs]),
            "rT": np.stack([bres["r_t"][b, :, h * 64:(h + 1) * 64] for b, h in bhs]),
            "vT": np.stack([bres["v_t"][b, :, h * 64:(h + 1) * 64] for b, h in bhs]),
            "ucol": np.concatenate([u[h].reshape(64, 1) for b, h in bhs], 0),
            "lnw": lnw, "lnb": lnb, "cmaskT": cmaskT, "idmb": idmb,
        }
        maps.append(m)
    return maps


def build_D():
    nc = bass.Bass("TRN2", target_bir_lowering=False)
    x_s = nc.dram_tensor("x_s", [TPC, D], F32, kind="ExternalInput").ap()
    o_s = nc.dram_tensor("o_s", [TPC, D], F32, kind="ExternalInput").ap()
    xnT_s = nc.dram_tensor("xnT_s", [D, TPC], F32, kind="ExternalInput").ap()
    WoT = nc.dram_tensor("WoT", [D, D], F32, kind="ExternalInput").ap()
    gwT = nc.dram_tensor("gwT", [D, 8], F32, kind="ExternalInput").ap()
    sgT = nc.dram_tensor("sgT", [D, 2816], F32, kind="ExternalInput").ap()
    suT = nc.dram_tensor("suT", [D, 2816], F32, kind="ExternalInput").ap()
    sdT = nc.dram_tensor("sdT", [2816, D], F32, kind="ExternalInput").ap()
    egT = nc.dram_tensor("egT", [8, D, 2816], F32, kind="ExternalInput").ap()
    euT = nc.dram_tensor("euT", [8, D, 2816], F32, kind="ExternalInput").ap()
    edT = nc.dram_tensor("edT", [8, 2816, D], F32, kind="ExternalInput").ap()
    idm_in = nc.dram_tensor("idm", [128, 128], F32, kind="ExternalInput").ap()
    out0_s = nc.dram_tensor("out0_s", [TPC, D], F32, kind="ExternalOutput").ap()
    acc_s = nc.dram_tensor("acc_s", [TPC, D], F32, kind="Internal").ap()
    wts_s = nc.dram_tensor("wts_s", [TPC, 8], F32, kind="Internal").ap()

    NG = TPC // 512
    HC = 11

    import contextlib
    st = contextlib.ExitStack()
    E = st.enter_context
    xnTb = E(nc.sbuf_tensor("xnTb", [128, 8 * 512], BF16))
    xnTf = E(nc.sbuf_tensor("xnTf", [128, 512], F32))
    wg2 = E(nc.sbuf_tensor("wg2", [128, 8 * 1408], BF16))
    wu2 = E(nc.sbuf_tensor("wu2", [128, 8 * 1408], BF16))
    wd2 = E(nc.sbuf_tensor("wd2", [128, HC * 1024], BF16))
    gwTs = E(nc.sbuf_tensor("gwTs", [128, 8 * 8], F32))
    WoTb = E(nc.sbuf_tensor("WoTb", [128, 8 * 1024], BF16))
    oTb = E(nc.sbuf_tensor("oTb", [128, 8 * 128], BF16))
    idms = E(nc.sbuf_tensor("idms", [128, 128], F32))
    hb = E(nc.sbuf_tensor("hb", [128, HC * 512], BF16))
    sg_t = E(nc.sbuf_tensor("sg_t", [128, 512], F32))
    acc = E(nc.sbuf_tensor("acc", [128, 1024], F32))
    tmp = E(nc.sbuf_tensor("tmp", [128, 1024], F32))
    wtst = E(nc.sbuf_tensor("wtst", [128, 8], F32))
    wcol = E(nc.sbuf_tensor("wcol", [128, 1], F32))
    m1 = E(nc.sbuf_tensor("m1", [128, 8], F32))
    mi = E(nc.sbuf_tensor("mi", [128, 8], U32))
    mu = E(nc.sbuf_tensor("mu", [128, 1], F32))
    psA = E(nc.psum_tensor("psA", [128, 512], F32))
    psB2 = E(nc.psum_tensor("psB2", [128, 512], F32))
    psC = E(nc.psum_tensor("psC", [128, 512], F32))
    psR = E(nc.psum_tensor("psR", [128, 8], F32))
    pso = E(nc.psum_tensor("pso", [128, 128], F32))
    sgs = E(nc.semaphore("sg"))
    sds = E(nc.semaphore("sd"))
    blk = E(nc.Block())

    p = Prog(nc)
    p.dma("sync", idms[:], idm_in[:])
    for c in range(8):
        p.gdma(gwTs[:, c * 8:(c + 1) * 8], gwT[c * 128:(c + 1) * 128, :])
        p.gdma(WoTb[:, c * 1024:(c + 1) * 1024], WoT[c * 128:(c + 1) * 128, :])

    def load_half(g_ap, u_ap, d_ap, half):
        h0 = half * 1408
        for c in range(8):
            p.gdma(wg2[:, c * 1408:(c + 1) * 1408], g_ap[c * 128:(c + 1) * 128, h0:h0 + 1408])
            p.gdma(wu2[:, c * 1408:(c + 1) * 1408], u_ap[c * 128:(c + 1) * 128, h0:h0 + 1408])
        for dc in range(HC):
            p.gdma(wd2[:, dc * 1024:(dc + 1) * 1024], d_ap[half * 1408 + dc * 128:half * 1408 + (dc + 1) * 128, :])

    def routing(t0):
        for c in range(8):
            p.dma("sync", xnTf[:, 0:128], xnT_s[c * 128:(c + 1) * 128, t0:t0 + 128])
            p.t(lambda e, c=c: e.matmul(out=psR[:], lhsT=xnTf[:, 0:128],
                                        rhs=gwTs[:, c * 8:(c + 1) * 8], start=(c == 0), stop=(c == 7)))
        p.v(lambda e: e.reduce_max(out=mu[:], in_=psR[:], axis=AX.X))
        p.v(lambda e: e.tensor_scalar(out=wtst[:], in0=psR[:], scalar1=mu[:], scalar2=None, op0=ALU.subtract))
        p.s(lambda e: e.activation(out=wtst[:], in_=wtst[:], func=AF.Exp))
        p.v(lambda e: e.reduce_sum(out=mu[:], in_=wtst[:], axis=AX.X))
        p.v(lambda e: e.reciprocal(out=mu[:], in_=mu[:]))
        p.v(lambda e: e.tensor_scalar(out=wtst[:], in0=wtst[:], scalar1=mu[:], scalar2=None, op0=ALU.mult))
        p.v(lambda e: e.max(out=m1[:], in_=wtst[:]))
        p.v(lambda e: e.tensor_scalar(out=tmp[:, 0:8], in0=wtst[:], scalar1=m1[:, 1:2], scalar2=None, op0=ALU.is_ge))
        p.v(lambda e: e.tensor_tensor(out=wtst[:], in0=wtst[:], in1=tmp[:, 0:8], op=ALU.mult))
        p.v(lambda e: e.tensor_tensor(out=mu[:], in0=m1[:, 0:1], in1=m1[:, 1:2], op=ALU.add))
        p.s(lambda e: e.activation(out=mu[:], in_=mu[:], func=AF.Copy, bias=1e-20))
        p.v(lambda e: e.reciprocal(out=mu[:], in_=mu[:]))
        p.v(lambda e: e.tensor_scalar(out=wtst[:], in0=wtst[:], scalar1=mu[:], scalar2=None, op0=ALU.mult))
        p.dma("sync", wts_s[t0:t0 + 128, :], wtst[:])

    def ffn_half_pass(gi, mode, e_i=None):
        t0 = gi * 512
        for c in range(8):
            p.dma("sync", xnTf[:], xnT_s[c * 128:(c + 1) * 128, t0:t0 + 512])
            p.v(lambda e, c=c: e.tensor_copy(out=xnTb[:, c * 512:(c + 1) * 512], in_=xnTf[:]))
        for dc in range(HC):
            def gmm(e, dc=dc):
                r = None
                for c in range(8):
                    r = e.matmul(out=psA[:], lhsT=wg2[:, c * 1408 + dc * 128:c * 1408 + (dc + 1) * 128],
                                 rhs=xnTb[:, c * 512:(c + 1) * 512], start=(c == 0), stop=(c == 7))
                return r
            p.op("tensor", gmm)
            def umm(e, dc=dc):
                r = None
                for c in range(8):
                    r = e.matmul(out=psB2[:], lhsT=wu2[:, c * 1408 + dc * 128:c * 1408 + (dc + 1) * 128],
                                 rhs=xnTb[:, c * 512:(c + 1) * 512], start=(c == 0), stop=(c == 7))
                return r
            p.op("tensor", umm)
            p.s(lambda e: e.activation(out=sg_t[:], in_=psA[:], func=AF.Silu))
            p.v(lambda e, dc=dc: e.tensor_tensor(out=hb[:, dc * 512:(dc + 1) * 512], in0=sg_t[:], in1=psB2[:], op=ALU.mult))
        for tc in range(4):
            t0c = t0 + tc * 128
            if mode == "base0":
                p.dma("sync", acc[:], x_s[t0c:t0c + 128, :])
                p.dma("sync", tmp[:], o_s[t0c:t0c + 128, :])
                for c in range(8):
                    p.t(lambda e, c=c: e.transpose(out=pso[:], in_=tmp[:, c * 128:(c + 1) * 128], identity=idms[:]))
                    p.v(lambda e, c=c: e.tensor_copy(out=oTb[:, c * 128:(c + 1) * 128], in_=pso[:]))
                for dh in range(2):
                    def wmm(e, dh=dh):
                        r = None
                        for c in range(8):
                            r = e.matmul(out=psC[:], lhsT=oTb[:, c * 128:(c + 1) * 128],
                                         rhs=WoTb[:, c * 1024 + dh * 512:c * 1024 + (dh + 1) * 512],
                                         start=(c == 0), stop=(c == 7))
                        return r
                    p.op("tensor", wmm)
                    p.v(lambda e, dh=dh: e.tensor_tensor(out=acc[:, dh * 512:(dh + 1) * 512],
                                                         in0=acc[:, dh * 512:(dh + 1) * 512], in1=psC[:], op=ALU.add))
            else:
                p.dma("sync", acc[:], acc_s[t0c:t0c + 128, :])
                if mode == "exp":
                    p.dma("sync", wcol[:], wts_s[t0c:t0c + 128, e_i:e_i + 1], allow_slow_non_contiguous=True)
            for dh in range(2):
                def dmm(e, tc=tc, dh=dh):
                    r = None
                    for dc in range(HC):
                        r = e.matmul(out=psC[:], lhsT=hb[:, dc * 512 + tc * 128:dc * 512 + (tc + 1) * 128],
                                     rhs=wd2[:, dc * 1024 + dh * 512:dc * 1024 + (dh + 1) * 512],
                                     start=(dc == 0), stop=(dc == HC - 1))
                    return r
                p.op("tensor", dmm)
                if mode == "exp":
                    p.v(lambda e, dh=dh: e.tensor_scalar(out=tmp[:, dh * 512:(dh + 1) * 512], in0=psC[:],
                                                         scalar1=wcol[:], scalar2=None, op0=ALU.mult))
                    p.v(lambda e, dh=dh: e.tensor_tensor(out=acc[:, dh * 512:(dh + 1) * 512],
                                                         in0=acc[:, dh * 512:(dh + 1) * 512],
                                                         in1=tmp[:, dh * 512:(dh + 1) * 512], op=ALU.add))
                else:
                    p.v(lambda e, dh=dh: e.tensor_tensor(out=acc[:, dh * 512:(dh + 1) * 512],
                                                         in0=acc[:, dh * 512:(dh + 1) * 512], in1=psC[:], op=ALU.add))
            dst = out0_s if last[0] else acc_s
            p.dma("sync", dst[t0c:t0c + 128, :], acc[:])

    last = [False]
    for tci in range(TPC // 128):
        routing(tci * 128)
    load_half(sgT, suT, sdT, 0)
    for gi in range(NG):
        ffn_half_pass(gi, "base0")
    load_half(sgT, suT, sdT, 1)
    for gi in range(NG):
        ffn_half_pass(gi, "shared1")
    for e_i in range(8):
        for half in range(2):
            load_half(egT[e_i], euT[e_i], edT[e_i], half)
            if e_i == 7 and half == 1:
                last[0] = True
            for gi in range(NG):
                ffn_half_pass(gi, "exp", e_i)
    p.finalize(blk, {"g": sgs, "gdma": sds})
    st.close()
    return nc


def host_D(inputs, x, o, xnT):
    tr = lambda a: np.ascontiguousarray(np.asarray(a, np.float32).swapaxes(-1, -2))
    WoT = tr(inputs["L0_attn_W_o"])
    gwT = tr(inputs["L0_ffn_gate_w"])
    sgT = tr(inputs["L0_ffn_shared_gate"])
    suT = tr(inputs["L0_ffn_shared_up"])
    sdT = tr(inputs["L0_ffn_shared_down"])
    egT = tr(inputs["L0_ffn_expert_gate"])
    euT = tr(inputs["L0_ffn_expert_up"])
    edT = tr(inputs["L0_ffn_expert_down"])
    idm = np.eye(128, dtype=np.float32)
    xf = x.reshape(TOK, D)
    of = o.reshape(TOK, D)
    maps = []
    for c in range(NCORE):
        lo = c * TPC
        b = lo // T
        maps.append({"x_s": xf[lo:lo + TPC], "o_s": of[lo:lo + TPC],
                     "xnT_s": np.ascontiguousarray(xnT[b, :, lo % T:lo % T + TPC]),
                     "WoT": WoT, "gwT": gwT, "sgT": sgT, "suT": suT, "sdT": sdT,
                     "egT": egT, "euT": euT, "edT": edT, "idm": idm})
    return maps


def build_E():
    nc = bass.Bass("TRN2", target_bir_lowering=False)
    out0_s = nc.dram_tensor("out0_s", [TPC, D], F32, kind="ExternalInput").ap()
    WT = {k: nc.dram_tensor("W" + k + "T", [D, D], F32, kind="ExternalInput").ap() for k in "qkv"}
    bias = {k: nc.dram_tensor("b" + k, [D, 1], F32, kind="ExternalInput").ap() for k in "qkv"}
    WprojT = nc.dram_tensor("WprojT", [D, 16], F32, kind="ExternalInput").ap()
    idm_in = nc.dram_tensor("idm", [128, 128], F32, kind="ExternalInput").ap()
    xn1_s = nc.dram_tensor("xn1_s", [TPC, D], F32, kind="ExternalOutput").ap()
    xn1T_s = nc.dram_tensor("xn1T_s", [D, TPC], F32, kind="ExternalOutput").ap()
    vC_s = nc.dram_tensor("vC_s", [D, TPC], F32, kind="ExternalOutput").ap()
    zq_s = nc.dram_tensor("zq_s", [16, TPC], F32, kind="ExternalOutput").ap()
    zk_s = nc.dram_tensor("zk_s", [16, TPC], F32, kind="ExternalOutput").ap()

    import contextlib
    st = contextlib.ExitStack()
    E = st.enter_context
    wb = E(nc.sbuf_tensor("wb", [128, 3 * 8 * D], BF16))
    wpj = E(nc.sbuf_tensor("wpj", [128, 8 * 16], F32))
    bcol = E(nc.sbuf_tensor("bcol", [128, 3 * 8], F32))
    idms = E(nc.sbuf_tensor("idms", [128, 128], F32))
    xt = E(nc.sbuf_tensor("xt", [128, D], F32))
    xn = E(nc.sbuf_tensor("xn", [128, D], F32))
    xTb = E(nc.sbuf_tensor("xTb", [128, 8 * 512], BF16))
    xTf = E(nc.sbuf_tensor("xTf", [128, 128], F32))
    qC = E(nc.sbuf_tensor("qC", [128, 8 * 512], F32))
    kC = E(nc.sbuf_tensor("kC", [128, 8 * 512], F32))
    zbuf = E(nc.sbuf_tensor("zbuf", [16, 512], F32))
    tmp = E(nc.sbuf_tensor("tmp", [128, D], F32))
    mu = E(nc.sbuf_tensor("mu", [128, 1], F32))
    var = E(nc.sbuf_tensor("var", [128, 1], F32))
    cb = E(nc.sbuf_tensor("cb", [128, 1], F32))
    psT = E(nc.psum_tensor("psT", [128, 128], F32))
    psQ = E(nc.psum_tensor("psQ", [128, 512], F32))
    psZ = E(nc.psum_tensor("psZ", [16, 512], F32))
    sgs = E(nc.semaphore("sg"))
    sds = E(nc.semaphore("sd"))
    blk = E(nc.Block())

    p = Prog(nc)
    p.dma("sync", idms[:], idm_in[:])
    p.g(lambda e: e.memset(cb[:], 1e-5))
    for i, k in enumerate("qkv"):
        for c in range(8):
            p.gdma(wb[:, (i * 8 + c) * D:(i * 8 + c + 1) * D], WT[k][c * 128:(c + 1) * 128, :])
            p.dma("sync", bcol[:, i * 8 + c:i * 8 + c + 1], bias[k][c * 128:(c + 1) * 128, :])
    for c in range(8):
        p.dma("sync", wpj[:, c * 16:(c + 1) * 16], WprojT[c * 128:(c + 1) * 128, :])

    NG = TPC // 512
    for gi in range(NG):
        t0 = gi * 512
        for tc in range(4):
            t0c = t0 + tc * 128
            p.dma("sync", xt[:], out0_s[t0c:t0c + 128, :])
            ln_tile(p, xn, xt, tmp, mu, var, cb[:, 0:1])
            p.dma("sync", xn1_s[t0c:t0c + 128, :], xn[:])
            for c in range(8):
                p.t(lambda e, c=c: e.transpose(out=psT[:], in_=xn[:, c * 128:(c + 1) * 128], identity=idms[:]))
                p.v(lambda e: e.tensor_copy(out=xTf[:], in_=psT[:]))
                p.v(lambda e, c=c, tc=tc: e.tensor_copy(out=xTb[:, c * 512 + tc * 128:c * 512 + (tc + 1) * 128], in_=xTf[:]))
                p.dma("sync", xn1T_s[c * 128:(c + 1) * 128, t0c:t0c + 128], xTf[:])
        for i, k in enumerate("qkv"):
            for oc in range(8):
                def qmm(e, i=i, oc=oc):
                    r = None
                    for c in range(8):
                        r = e.matmul(out=psQ[:], lhsT=wb[:, (i * 8 + c) * D + oc * 128:(i * 8 + c) * D + oc * 128 + 128],
                                     rhs=xTb[:, c * 512:(c + 1) * 512], start=(c == 0), stop=(c == 7))
                    return r
                p.op("tensor", qmm)
                if k == "v":
                    p.v(lambda e, i=i, oc=oc: e.tensor_scalar(out=tmp[:, 0:512], in0=psQ[:], scalar1=bcol[:, i * 8 + oc:i * 8 + oc + 1], scalar2=None, op0=ALU.add))
                    p.dma("sync", vC_s[oc * 128:(oc + 1) * 128, t0:t0 + 512], tmp[:, 0:512])
                else:
                    dstC = qC if k == "q" else kC
                    p.v(lambda e, i=i, oc=oc, dstC=dstC: e.tensor_scalar(out=dstC[:, oc * 512:(oc + 1) * 512], in0=psQ[:], scalar1=bcol[:, i * 8 + oc:i * 8 + oc + 1], scalar2=None, op0=ALU.add))
        for k in "qk":
            srcC = qC if k == "q" else kC
            def zmm(e, srcC=srcC):
                r = None
                for c in range(8):
                    r = e.matmul(out=psZ[:], lhsT=wpj[:, c * 16:(c + 1) * 16],
                                 rhs=srcC[:, c * 512:(c + 1) * 512], start=(c == 0), stop=(c == 7))
                return r
            p.op("tensor", zmm)
            p.v(lambda e: e.tensor_copy(out=zbuf[:], in_=psZ[:]))
            dst = zq_s if k == "q" else zk_s
            p.dma("sync", dst[:, t0:t0 + 512], zbuf[:])
    p.finalize(blk, {"g": sgs, "gdma": sds})
    st.close()
    return nc


def host_E(inputs, out0):
    tr = lambda a: np.ascontiguousarray(np.asarray(a, np.float32).swapaxes(-1, -2))
    idm = np.eye(128, dtype=np.float32)
    of = out0.reshape(TOK, D)
    maps = []
    for c in range(NCORE):
        lo = c * TPC
        m = {"out0_s": of[lo:lo + TPC], "WprojT": tr(inputs["L1_attn_Wproj"]), "idm": idm}
        for k, nm in [("q", "Wq"), ("k", "Wk"), ("v", "Wv")]:
            m["W" + k + "T"] = tr(inputs["L1_attn_" + nm])
            m["b" + k] = np.asarray(inputs["L1_attn_b" + k], np.float32).reshape(D, 1)
        maps.append(m)
    return maps


def build_G():
    nc = bass.Bass("TRN2", target_bir_lowering=False)
    out0_s = nc.dram_tensor("out0_s", [TPC, D], F32, kind="ExternalInput").ap()
    ctx_s = nc.dram_tensor("ctx_s", [TPC, D], F32, kind="ExternalInput").ap()
    xn1_s = nc.dram_tensor("xn1_s", [TPC, D], F32, kind="ExternalInput").ap()
    xn1T_s = nc.dram_tensor("xn1T_s", [D, TPC], F32, kind="ExternalInput").ap()
    gate_W = nc.dram_tensor("gate_W", [D, 8], F32, kind="ExternalInput").ap()
    W1i = nc.dram_tensor("W1", [8, D, 2048], F32, kind="ExternalInput").ap()
    b1i = nc.dram_tensor("b1", [8, 2048, 1], F32, kind="ExternalInput").ap()
    W2i = nc.dram_tensor("W2", [8, 2048, D], F32, kind="ExternalInput").ap()
    b2i = nc.dram_tensor("b2", [8, 1, D], F32, kind="ExternalInput").ap()
    y_s = nc.dram_tensor("y_s", [TPC, D], F32, kind="ExternalOutput").ap()
    acc_s = nc.dram_tensor("acc_s", [TPC, D], F32, kind="Internal").ap()
    wts_s = nc.dram_tensor("wts_s", [TPC, 8], F32, kind="Internal").ap()

    import contextlib
    st = contextlib.ExitStack()
    E = st.enter_context
    xnTb = E(nc.sbuf_tensor("xnTb", [128, 8 * 512], BF16))
    xnTf = E(nc.sbuf_tensor("xnTf", [128, 512], F32))
    w1b = E(nc.sbuf_tensor("w1b", [128, 8 * 2048], BF16))
    w2b = E(nc.sbuf_tensor("w2b", [128, 16 * 1024], BF16))
    b1c = E(nc.sbuf_tensor("b1c", [128, 16], F32))
    b2B = E(nc.sbuf_tensor("b2B", [128, 1024], F32))
    gws = E(nc.sbuf_tensor("gws", [128, 8 * 8], F32))
    hb = E(nc.sbuf_tensor("hb", [128, 16 * 512], BF16))
    acc = E(nc.sbuf_tensor("acc", [128, 1024], F32))
    tmp = E(nc.sbuf_tensor("tmp", [128, 1024], F32))
    wtst = E(nc.sbuf_tensor("wtst", [128, 8], F32))
    wcol = E(nc.sbuf_tensor("wcol", [128, 1], F32))
    m1 = E(nc.sbuf_tensor("m1", [128, 8], F32))
    mu = E(nc.sbuf_tensor("mu", [128, 1], F32))
    psA = E(nc.psum_tensor("psA", [128, 512], F32))
    psC = E(nc.psum_tensor("psC", [128, 512], F32))
    psR = E(nc.psum_tensor("psR", [128, 8], F32))
    sgs = E(nc.semaphore("sg"))
    sds = E(nc.semaphore("sd"))
    blk = E(nc.Block())

    p = Prog(nc)
    for c in range(8):
        p.gdma(gws[:, c * 8:(c + 1) * 8], gate_W[c * 128:(c + 1) * 128, :])

    def routing(t0):
        for c in range(8):
            p.dma("sync", xnTf[:, 0:128], xn1T_s[c * 128:(c + 1) * 128, t0:t0 + 128])
            p.t(lambda e, c=c: e.matmul(out=psR[:], lhsT=xnTf[:, 0:128],
                                        rhs=gws[:, c * 8:(c + 1) * 8], start=(c == 0), stop=(c == 7)))
        p.v(lambda e: e.reduce_max(out=mu[:], in_=psR[:], axis=AX.X))
        p.v(lambda e: e.tensor_scalar(out=wtst[:], in0=psR[:], scalar1=mu[:], scalar2=None, op0=ALU.subtract))
        p.s(lambda e: e.activation(out=wtst[:], in_=wtst[:], func=AF.Exp))
        p.v(lambda e: e.reduce_sum(out=mu[:], in_=wtst[:], axis=AX.X))
        p.v(lambda e: e.reciprocal(out=mu[:], in_=mu[:]))
        p.v(lambda e: e.tensor_scalar(out=wtst[:], in0=wtst[:], scalar1=mu[:], scalar2=None, op0=ALU.mult))
        p.v(lambda e: e.max(out=m1[:], in_=wtst[:]))
        p.v(lambda e: e.tensor_scalar(out=tmp[:, 0:8], in0=wtst[:], scalar1=m1[:, 1:2], scalar2=None, op0=ALU.is_ge))
        p.v(lambda e: e.tensor_tensor(out=wtst[:], in0=wtst[:], in1=tmp[:, 0:8], op=ALU.mult))
        p.v(lambda e: e.tensor_tensor(out=mu[:], in0=m1[:, 0:1], in1=m1[:, 1:2], op=ALU.add))
        p.s(lambda e: e.activation(out=mu[:], in_=mu[:], func=AF.Copy, bias=1e-8))
        p.v(lambda e: e.reciprocal(out=mu[:], in_=mu[:]))
        p.v(lambda e: e.tensor_scalar(out=wtst[:], in0=wtst[:], scalar1=mu[:], scalar2=None, op0=ALU.mult))
        p.dma("sync", wts_s[t0:t0 + 128, :], wtst[:])

    NG = TPC // 512
    for tci in range(TPC // 128):
        routing(tci * 128)
    for tci in range(TPC // 128):
        t0c = tci * 128
        p.dma("sync", acc[:], out0_s[t0c:t0c + 128, :])
        p.dma("sync", tmp[:], ctx_s[t0c:t0c + 128, :])
        p.v(lambda e: e.tensor_tensor(out=acc[:], in0=acc[:], in1=tmp[:], op=ALU.add))
        p.dma("sync", tmp[:], xn1_s[t0c:t0c + 128, :])
        p.v(lambda e: e.tensor_tensor(out=acc[:], in0=acc[:], in1=tmp[:], op=ALU.add))
        p.dma("sync", acc_s[t0c:t0c + 128, :], acc[:])
    for e_i in range(8):
        for c in range(8):
            p.gdma(w1b[:, c * 2048:(c + 1) * 2048], W1i[e_i, c * 128:(c + 1) * 128, :])
        for dc in range(16):
            p.gdma(w2b[:, dc * 1024:(dc + 1) * 1024], W2i[e_i, dc * 128:(dc + 1) * 128, :])
            p.dma("sync", b1c[:, dc:dc + 1], b1i[e_i, dc * 128:(dc + 1) * 128, :])
        p.dma("sync", b2B[:], b2i[e_i, 0:1, :].broadcast_to((128, 1024)))
        for gi in range(NG):
            t0 = gi * 512
            for c in range(8):
                p.dma("sync", xnTf[:], xn1T_s[c * 128:(c + 1) * 128, t0:t0 + 512])
                p.v(lambda e, c=c: e.tensor_copy(out=xnTb[:, c * 512:(c + 1) * 512], in_=xnTf[:]))
            for dc in range(16):
                def gmm(e, dc=dc):
                    r = None
                    for c in range(8):
                        r = e.matmul(out=psA[:], lhsT=w1b[:, c * 2048 + dc * 128:c * 2048 + (dc + 1) * 128],
                                     rhs=xnTb[:, c * 512:(c + 1) * 512], start=(c == 0), stop=(c == 7))
                    return r
                p.op("tensor", gmm)
                p.s(lambda e, dc=dc: e.activation(out=hb[:, dc * 512:(dc + 1) * 512], in_=psA[:],
                                                  func=AF.Relu, bias=b1c[:, dc:dc + 1]))
            for tc in range(4):
                t0c = t0 + tc * 128
                p.dma("sync", acc[:], acc_s[t0c:t0c + 128, :])
                p.dma("sync", wcol[:], wts_s[t0c:t0c + 128, e_i:e_i + 1], allow_slow_non_contiguous=True)
                for dh in range(2):
                    def dmm(e, tc=tc, dh=dh):
                        r = None
                        for dc in range(16):
                            r = e.matmul(out=psC[:], lhsT=hb[:, dc * 512 + tc * 128:dc * 512 + (tc + 1) * 128],
                                         rhs=w2b[:, dc * 1024 + dh * 512:dc * 1024 + (dh + 1) * 512],
                                         start=(dc == 0), stop=(dc == 15))
                        return r
                    p.op("tensor", dmm)
                    p.v(lambda e, dh=dh: e.tensor_tensor(out=tmp[:, dh * 512:(dh + 1) * 512], in0=psC[:],
                                                         in1=b2B[:, dh * 512:(dh + 1) * 512], op=ALU.add))
                    p.v(lambda e, dh=dh: e.tensor_scalar(out=tmp[:, dh * 512:(dh + 1) * 512],
                                                         in0=tmp[:, dh * 512:(dh + 1) * 512],
                                                         scalar1=wcol[:], scalar2=None, op0=ALU.mult))
                    p.v(lambda e, dh=dh: e.tensor_tensor(out=acc[:, dh * 512:(dh + 1) * 512],
                                                         in0=acc[:, dh * 512:(dh + 1) * 512],
                                                         in1=tmp[:, dh * 512:(dh + 1) * 512], op=ALU.add))
                dst = y_s if e_i == 7 else acc_s
                p.dma("sync", dst[t0c:t0c + 128, :], acc[:])
    p.finalize(blk, {"g": sgs, "gdma": sds})
    st.close()
    return nc


def host_G(inputs, out0, ctx, xn1, xn1T):
    of = out0.reshape(TOK, D)
    cf = ctx.reshape(TOK, D)
    xf = xn1.reshape(TOK, D)
    maps = []
    for c in range(NCORE):
        lo = c * TPC
        b = lo // T
        maps.append({
            "out0_s": of[lo:lo + TPC], "ctx_s": cf[lo:lo + TPC], "xn1_s": xf[lo:lo + TPC],
            "xn1T_s": np.ascontiguousarray(xn1T[b, :, lo % T:lo % T + TPC]),
            "gate_W": np.asarray(inputs["L1_ffn_gate_W"], np.float32),
            "W1": np.asarray(inputs["L1_ffn_W1"], np.float32),
            "b1": np.asarray(inputs["L1_ffn_b1"], np.float32).reshape(8, 2048, 1),
            "W2": np.asarray(inputs["L1_ffn_W2"], np.float32),
            "b2": np.asarray(inputs["L1_ffn_b2"], np.float32).reshape(8, 1, 1024),
        })
    return maps


def _stages_sort(nbits):
    st = []
    for P in range(nbits):
        st.append(("mirror", 1 << (P + 1)))
        for k in range(P - 1, -1, -1):
            st.append(("xor", 1 << k))
    return st


def _stages_merge(n):
    st = []
    d = n // 2
    while d >= 1:
        st.append(("xor", d))
        d //= 2
    return st


def build_F(Tm=T, CHK=1024):
    from concourse.bass import IndirectOffsetOnAxis
    nn = 2 * Tm
    FREE = nn // 128
    F2 = 2 * FREE
    NB = nn.bit_length() - 1
    stS = _stages_sort(NB)
    stM = _stages_merge(2 * nn)
    NT128 = nn // 128
    nc = bass.Bass("TRN2", target_bir_lowering=False)
    zKi = nc.dram_tensor("zK", [4, Tm], F32, kind="ExternalInput").ap()
    zQi = nc.dram_tensor("zQ", [4, Tm], F32, kind="ExternalInput").ap()
    vTi = nc.dram_tensor("vT", [4 * Tm, 64], F32, kind="ExternalInput").ap()
    lbwi = nc.dram_tensor("lbw", [4, 1], F32, kind="ExternalInput").ap()
    BXi = nc.dram_tensor("BX", [7, 128, 128], F32, kind="ExternalInput").ap()
    RSi = nc.dram_tensor("RS", [7, 128, 128], F32, kind="ExternalInput").ap()
    pmxi = nc.dram_tensor("pmx", [7, 128, F2], U8, kind="ExternalInput").ap()
    pmmi = nc.dram_tensor("pmm", [7, 128, F2], U8, kind="ExternalInput").ap()
    iotsi = nc.dram_tensor("iota_s", [128, FREE], F32, kind="ExternalInput").ap()
    iotmi = nc.dram_tensor("iota_m", [128, F2], F32, kind="ExternalInput").ap()
    UT_in = nc.dram_tensor("UT", [128, 128], F32, kind="ExternalInput").ap()
    LT_in = nc.dram_tensor("LT", [128, 128], F32, kind="ExternalInput").ap()
    e127_in = nc.dram_tensor("e127", [128, 1], F32, kind="ExternalInput").ap()
    ctxTo = nc.dram_tensor("ctxT", [4, Tm, 64], F32, kind="ExternalOutput").ap()
    zdram = nc.dram_tensor("zdram", [1, nn], F32, kind="Internal").ap()
    idx_d = nc.dram_tensor("idx_d", [1, nn], U32, kind="Internal").ap()
    rank_d = nc.dram_tensor("rank_d", [1, nn], U32, kind="Internal").ap()
    l_d = nc.dram_tensor("l_d", [1, nn], U32, kind="Internal").ap()
    r_d = nc.dram_tensor("r_d", [1, nn], U32, kind="Internal").ap()
    tb_d = nc.dram_tensor("tb_d", [nn, 128], F32, kind="Internal").ap()
    ctx_d = nc.dram_tensor("ctx_d", [nn, 64], F32, kind="Internal").ap()

    import contextlib
    st_ = contextlib.ExitStack()
    E = st_.enter_context
    Xa = E(nc.sbuf_tensor("Xa", [128, FREE], F32))
    Xb2 = E(nc.sbuf_tensor("Xb2", [128, FREE], F32))
    Ua = E(nc.sbuf_tensor("Ua", [128, F2], F32))
    Ub = E(nc.sbuf_tensor("Ub", [128, F2], F32))
    pma = E(nc.sbuf_tensor("pma", [128, F2], F32))
    pmb = E(nc.sbuf_tensor("pmb", [128, F2], F32))
    M1m = E(nc.sbuf_tensor("M1m", [128, len(stS) * FREE], U8))
    UMm = E(nc.sbuf_tensor("UMm", [128, len(stM) * F2], U8))
    Mg = E(nc.sbuf_tensor("Mg", [128, F2], U8))
    Ml = E(nc.sbuf_tensor("Ml", [128, F2], U8))
    BXs = E(nc.sbuf_tensor("BXs", [128, 7 * 128], F32))
    RSs = E(nc.sbuf_tensor("RSs", [128, 7 * 128], F32))
    pmxs = E(nc.sbuf_tensor("pmxs", [128, 7 * F2], U8))
    pmms = E(nc.sbuf_tensor("pmms", [128, 7 * F2], U8))
    iots = E(nc.sbuf_tensor("iots", [128, FREE], F32))
    iotm = E(nc.sbuf_tensor("iotm", [128, F2], F32))
    UTs = E(nc.sbuf_tensor("UTs", [128, 128], F32))
    LTs = E(nc.sbuf_tensor("LTs", [128, 128], F32))
    e127 = E(nc.sbuf_tensor("e127s", [128, 1], F32))
    ccol = E(nc.sbuf_tensor("ccol", [128, 8], F32))
    u32a = E(nc.sbuf_tensor("u32a", [128, F2], U32))
    zerF2 = E(nc.sbuf_tensor("zerF2", [128, F2], F32))
    zer64 = E(nc.sbuf_tensor("zer64t", [128, 128], F32))
    ixc = E(nc.sbuf_tensor("ixc", [128, 3], U32))
    zcol = E(nc.sbuf_tensor("zcol", [128, 1], F32))
    vtile = E(nc.sbuf_tensor("vtile", [128, 128], F32))
    GLt = E(nc.sbuf_tensor("GLt", [128, 128], F32))
    GRt = E(nc.sbuf_tensor("GRt", [128, 128], F32))
    PPt = E(nc.sbuf_tensor("PPt", [128, 128], F32))
    s1 = E(nc.sbuf_tensor("s1t", [128, 64], F32))
    s2 = E(nc.sbuf_tensor("s2t", [128, 64], F32))
    s3 = E(nc.sbuf_tensor("s3t", [128, 64], F32))
    otile = E(nc.sbuf_tensor("otile", [128, 128], F32))
    carry = E(nc.sbuf_tensor("carryF", [1, 128], F32))
    psB = E(nc.psum_tensor("psB", [128, F2], F32))
    psP = E(nc.psum_tensor("psPf", [128, 128], F32))
    psC1 = E(nc.psum_tensor("psC1", [1, 128], F32))
    sgs = E(nc.semaphore("sg"))
    sds = E(nc.semaphore("sd"))
    blk = E(nc.Block())

    p = Prog(nc)
    p.dma("sync", iots[:], iotsi[:])
    p.dma("sync", iotm[:], iotmi[:])
    p.dma("sync", UTs[:], UT_in[:])
    p.dma("sync", LTs[:], LT_in[:])
    p.dma("sync", e127[:], e127_in[:])
    for i in range(7):
        p.dma("sync", BXs[:, i * 128:(i + 1) * 128], BXi[i])
        p.dma("sync", RSs[:, i * 128:(i + 1) * 128], RSi[i])
        p.dma("sync", pmxs[:, i * F2:(i + 1) * F2], pmxi[i])
        p.dma("sync", pmms[:, i * F2:(i + 1) * F2], pmmi[i])
    p.g(lambda e: e.memset(ccol[:, 0:1], 1.0))
    p.g(lambda e: e.memset(zerF2[:], 0.0))
    p.g(lambda e: e.memset(zer64[:], 0.0))
    p.g(lambda e: e.memset(pma[:], 0.0))
    p.g(lambda e: e.memset(pmb[:], 0.0))

    def stage_ops(kind, param, FR, cur, new, mask_ap, record):
        if (kind == "xor" and param < FR) or (kind == "mirror" and param <= FR):
            if kind == "xor":
                d = param
                rc = cur[:].rearrange("p (a t b) -> p a t b", t=2, b=d)
                rn = new[:].rearrange("p (a t b) -> p a t b", t=2, b=d)
                rm = mask_ap.rearrange("p (a t b) -> p a t b", t=2, b=d)
                a, b_ = rc[:, :, 0:1, :], rc[:, :, 1:2, :]
                na, nb = rn[:, :, 0:1, :], rn[:, :, 1:2, :]
                ma = rm[:, :, 0:1, :]
            else:
                L2 = param
                rc = cur[:].rearrange("p (s l) -> p s l", l=L2)
                rn = new[:].rearrange("p (s l) -> p s l", l=L2)
                rm = mask_ap.rearrange("p (s l) -> p s l", l=L2)
                a, b_ = rc[:, :, 0:L2 // 2], rc[:, :, L2 - 1:L2 // 2 - 1:-1]
                na, nb = rn[:, :, 0:L2 // 2], rn[:, :, L2 - 1:L2 // 2 - 1:-1]
                ma = rm[:, :, 0:L2 // 2]
            if record:
                p.v(lambda e: e.tensor_tensor(out=ma, in0=a, in1=b_, op=ALU.is_gt))
            p.v(lambda e: e.select(out=na, mask=ma, on_true=b_, on_false=a, add_drain=True))
            p.v(lambda e: e.select(out=nb, mask=ma, on_true=a, on_false=b_, add_drain=True))
        else:
            if kind == "xor":
                Dp = param // FR
                perm = BXs[:, (Dp.bit_length() - 1) * 128:(Dp.bit_length() - 1) * 128 + 128]
                pmask = pmxs[:, (Dp.bit_length() - 1) * F2:(Dp.bit_length() - 1) * F2 + FR]
                partner = lambda: psB[:, 0:FR]
            else:
                Sp = param // FR
                perm = RSs[:, (Sp.bit_length() - 2) * 128:(Sp.bit_length() - 2) * 128 + 128]
                pmask = pmms[:, (Sp.bit_length() - 2) * F2:(Sp.bit_length() - 2) * F2 + FR]
                partner = lambda: psB[:, FR - 1::-1]
            p.t(lambda e: e.matmul(out=psB[:, 0:FR], lhsT=perm, rhs=cur[:], start=True, stop=True))
            if record:
                p.v(lambda e: e.tensor_tensor(out=Mg[:, 0:FR], in0=cur[:], in1=partner(), op=ALU.is_gt))
                p.v(lambda e: e.tensor_tensor(out=Ml[:, 0:FR], in0=cur[:], in1=partner(), op=ALU.is_lt))
                p.v(lambda e: e.select(out=mask_ap, mask=pmask, on_true=Ml[:, 0:FR], on_false=Mg[:, 0:FR], add_drain=True))
            p.v(lambda e: e.select(out=new[:], mask=mask_ap, on_true=partner(), on_false=cur[:], add_drain=True))

    def run_net(stages, FR, buf0, buf1, mtensor, record, reverse=False):
        order = list(enumerate(stages))
        if reverse:
            order = order[::-1]
        cur, new = buf0, buf1
        for i, (kind, param) in order:
            mask_ap = mtensor[:, i * FR:(i + 1) * FR]
            stage_ops(kind, param, FR, cur, new, mask_ap, record)
            cur, new = new, cur
        return cur

    def dump_u32(src_ap, P, FR, dram_flat_ap):
        p.v(lambda e: e.tensor_copy(out=u32a[0:P, 0:FR], in_=src_ap))
        p.dma("sync", dram_flat_ap.rearrange("(p f) -> p f", p=P), u32a[0:P, 0:FR])

    _regs = {}

    def idma(out_ap, table_ap, ix_ap, bound, eoff=0):
        def fn(e):
            if bound not in _regs:
                _regs[bound] = e.alloc_register(name="bnd%d" % bound)
                e.reg_mov(_regs[bound], bound)
            return e.indirect_dma_start(out_ap, None, table_ap, IndirectOffsetOnAxis(ix_ap, 0),
                                        element_offset=eoff, bounds_check=_regs[bound], oob_is_err=False)
        p.op("gpsimd", fn, inc=16, sem="gdma")

    for k4 in range(4):
        p.dma("sync", Xa[0:64, :], zKi[k4].rearrange("(p f) -> p f", p=64))
        p.dma("sync", Xa[64:128, :], zQi[k4].rearrange("(p f) -> p f", p=64))
        cur = run_net(stS, FREE, Xa, Xb2, M1m, True)
        p.dma("sync", zdram[0, :].rearrange("(p f) -> p f", p=128), cur[:])
        oth = Xb2 if cur is Xa else Xa
        p.v(lambda e: e.tensor_copy(out=oth[:], in_=iots[:]))
        idxbuf = run_net(stS, FREE, oth, cur, M1m, False)
        dump_u32(idxbuf[:], 128, FREE, idx_d[0, :])
        oth2 = Xa if idxbuf is Xb2 else Xb2
        p.v(lambda e: e.tensor_copy(out=oth2[:], in_=iots[:]))
        rnkbuf = run_net(stS, FREE, oth2, idxbuf, M1m, False, reverse=True)
        dump_u32(rnkbuf[:], 128, FREE, rank_d[0, :])
        p.dma("sync", ccol[:, 1:2], lbwi[k4:k4 + 1, :].broadcast_to((128, 1)))
        p.s(lambda e: e.activation(out=ccol[:, 2:3], in_=ccol[:, 1:2], func=AF.Abs))
        p.s(lambda e: e.activation(out=ccol[:, 2:3], in_=ccol[:, 2:3], func=AF.Exp, scale=-1.0))
        p.s(lambda e: e.activation(out=ccol[:, 2:3], in_=ccol[:, 2:3], func=AF.Ln, bias=ccol[:, 0:1]))
        p.s(lambda e: e.activation(out=ccol[:, 3:4], in_=ccol[:, 1:2], func=AF.Relu))
        p.v(lambda e: e.tensor_tensor(out=ccol[:, 3:4], in0=ccol[:, 3:4], in1=ccol[:, 2:3], op=ALU.add))
        p.s(lambda e: e.activation(out=ccol[:, 3:4], in_=ccol[:, 3:4], func=AF.Copy, bias=1e-4))
        p.v(lambda e: e.reciprocal(out=ccol[:, 4:5], in_=ccol[:, 3:4]))
        for side in range(2):
            p.dma("sync", Ua[0:64, :], zdram[0, :].rearrange("(p f) -> p f", p=64))
            p.v(lambda e: e.tensor_copy(out=pmb[0:64, :], in_=Ua[0:64, F2 - 1::-1]))
            p.t(lambda e: e.matmul(out=psB[:], lhsT=RSs[:, 6 * 128:7 * 128], rhs=pmb[:], start=True, stop=True))
            p.v(lambda e: e.tensor_copy(out=Ua[64:128, :], in_=psB[64:128, :]))
            op = ALU.subtract if side == 0 else ALU.add
            p.v(lambda e, op=op: e.tensor_scalar(out=Ua[64:128, :], in0=Ua[64:128, :], scalar1=ccol[64:128, 3:4], scalar2=None, op0=op))
            curU = run_net(stM, F2, Ua, Ub, UMm, True)
            othU = Ub if curU is Ua else Ua
            p.g(lambda e: e.memset(pma[0:64, :], 0.0))
            p.g(lambda e: e.memset(pma[64:128, :], 1.0))
            flg = run_net(stM, F2, pma, pmb, UMm, False)
            p.v(lambda e, flg=flg, othU=othU: e.tensor_tensor_scan(out=othU[:], data0=flg[:], data1=zerF2[:], initial=0.0, op0=ALU.add, op1=ALU.add))
            flg = othU
            p.v(lambda e, flg=flg: e.tensor_copy(out=ccol[:, 5:6], in_=flg[:, F2 - 1:F2]))
            def carrymm(e):
                return e.matmul(out=psB[:, 0:1], lhsT=UTs[:], rhs=ccol[:, 5:6], start=True, stop=True)
            p.op("tensor", carrymm)
            p.v(lambda e, flg=flg: e.tensor_scalar(out=flg[:], in0=flg[:], scalar1=psB[:, 0:1], scalar2=None, op0=ALU.add))
            p.v(lambda e, flg=flg: e.tensor_tensor(out=flg[:], in0=iotm[:], in1=flg[:], op=ALU.subtract))
            vb = run_net(stM, F2, flg, (Ua if flg is Ub else Ub), UMm, False, reverse=True)
            p.v(lambda e, vb=vb: e.tensor_copy(out=pma[64:128, :], in_=vb[64:128, F2 - 1::-1]))
            p.t(lambda e: e.matmul(out=psB[:], lhsT=RSs[:, 6 * 128:7 * 128], rhs=pma[:], start=True, stop=True))
            p.v(lambda e: e.tensor_copy(out=pmb[0:64, :], in_=psB[0:64, :]))
            if side == 0:
                p.v(lambda e: e.tensor_scalar(out=pma[0:64, :], in0=pmb[0:64, :], scalar1=0.5, scalar2=float(nn + 5), op0=ALU.is_lt, op1=ALU.mult))
                p.v(lambda e: e.tensor_scalar(out=pmb[0:64, :], in0=pmb[0:64, :], scalar1=-1.0, scalar2=None, op0=ALU.add))
                p.v(lambda e: e.tensor_tensor(out=pmb[0:64, :], in0=pmb[0:64, :], in1=pma[0:64, :], op=ALU.add))
                dump_u32(pmb[0:64, :], 64, F2, l_d[0, :])
            else:
                p.v(lambda e: e.tensor_scalar(out=pmb[0:64, :], in0=pmb[0:64, :], scalar1=-1.0, scalar2=None, op0=ALU.add))
                dump_u32(pmb[0:64, :], 64, F2, r_d[0, :])
        p.g(lambda e: e.memset(carry[:], 0.0))
        for mt in range(NT128):
            m0 = mt * 128
            p.dma("sync", ixc[:, 0:1], idx_d[0, m0:m0 + 128].rearrange("(p o) -> p o", o=1))
            p.dma("sync", zcol[:], zdram[0, m0:m0 + 128].rearrange("(p o) -> p o", o=1))
            p.v(lambda e: e.tensor_copy(out=vtile[:, 0:64], in_=zer64[:, 0:64]))
            idma(vtile[:, 0:64], vTi, ixc[:, 0:1], Tm - 1, eoff=k4 * Tm * 64)
            p.v(lambda e: e.tensor_scalar(out=vtile[:, 64:128], in0=vtile[:, 0:64], scalar1=zcol[:], scalar2=None, op0=ALU.mult))
            def pmm_(e):
                e.matmul(out=psP[:], lhsT=LTs[:], rhs=vtile[:], start=True, stop=False)
                return e.matmul(out=psP[:], lhsT=ccol[0:1, 0:1].broadcast_to((1, 128)), rhs=carry[:], start=False, stop=True)
            p.op("tensor", pmm_)
            p.v(lambda e: e.tensor_copy(out=otile[:], in_=psP[:]))
            p.dma("sync", tb_d[m0:m0 + 128, :], otile[:])
            def cmm_(e):
                return e.matmul(out=psC1[:], lhsT=e127[:], rhs=otile[:], start=True, stop=True)
            p.op("tensor", cmm_)
            p.v(lambda e: e.tensor_copy(out=carry[:], in_=psC1[:]))
        for mt in range(NT128):
            m0 = mt * 128
            p.dma("sync", ixc[:, 1:2], l_d[0, m0:m0 + 128].rearrange("(p o) -> p o", o=1))
            p.dma("sync", ixc[:, 2:3], r_d[0, m0:m0 + 128].rearrange("(p o) -> p o", o=1))
            p.dma("sync", zcol[:], zdram[0, m0:m0 + 128].rearrange("(p o) -> p o", o=1))
            p.dma("sync", PPt[:], tb_d[m0:m0 + 128, :])
            p.v(lambda e: e.tensor_copy(out=GLt[:], in_=zer64[:]))
            idma(GLt[:], tb_d, ixc[:, 1:2], nn - 1)
            idma(GRt[:], tb_d, ixc[:, 2:3], nn - 1)
            p.v(lambda e: e.tensor_tensor(out=s1[:], in0=GRt[:, 0:64], in1=GLt[:, 0:64], op=ALU.subtract))
            p.v(lambda e: e.tensor_tensor(out=s2[:], in0=GRt[:, 0:64], in1=GLt[:, 0:64], op=ALU.add))
            p.v(lambda e: e.tensor_scalar(out=otile[:, 0:64], in0=PPt[:, 0:64], scalar1=-2.0, scalar2=None, op0=ALU.mult))
            p.v(lambda e: e.tensor_tensor(out=s2[:], in0=s2[:], in1=otile[:, 0:64], op=ALU.add))
            p.v(lambda e: e.tensor_tensor(out=s3[:], in0=GLt[:, 64:128], in1=GRt[:, 64:128], op=ALU.add))
            p.v(lambda e: e.tensor_scalar(out=otile[:, 64:128], in0=PPt[:, 64:128], scalar1=2.0, scalar2=None, op0=ALU.mult))
            p.v(lambda e: e.tensor_tensor(out=s3[:], in0=otile[:, 64:128], in1=s3[:], op=ALU.subtract))
            p.v(lambda e: e.tensor_scalar(out=zcol[:], in0=zcol[:], scalar1=ccol[:, 4:5], scalar2=None, op0=ALU.mult))
            p.v(lambda e: e.tensor_scalar(out=s2[:], in0=s2[:], scalar1=zcol[:], scalar2=None, op0=ALU.mult))
            p.v(lambda e: e.tensor_tensor(out=s1[:], in0=s1[:], in1=s2[:], op=ALU.add))
            p.v(lambda e: e.tensor_scalar(out=s3[:], in0=s3[:], scalar1=ccol[:, 4:5], scalar2=None, op0=ALU.mult))
            p.v(lambda e: e.tensor_tensor(out=s1[:], in0=s1[:], in1=s3[:], op=ALU.add))
            p.dma("sync", ctx_d[m0:m0 + 128, :], s1[:])
        for qt in range(Tm // 128):
            q0 = Tm + qt * 128
            p.dma("sync", ixc[:, 0:1], rank_d[0, q0:q0 + 128].rearrange("(p o) -> p o", o=1))
            idma(s1[:], ctx_d, ixc[:, 0:1], nn - 1)
            p.s(lambda e: e.mul(out=s1[:], in_=s1[:], mul=1.0 / Tm))
            p.dma("sync", ctxTo[k4, qt * 128:(qt + 1) * 128, :], s1[:])
    p.finalize(blk, {"g": sgs, "gdma": sds})
    st_.close()
    return nc


def host_F(inputs, zK, zQ, vC, Tm=T):
    nn = 2 * Tm
    FREE = nn // 128
    F2 = 2 * FREE
    pp = np.arange(128)
    BX = np.zeros((7, 128, 128), np.float32)
    RS = np.zeros((7, 128, 128), np.float32)
    pmx = np.zeros((7, 128, F2), np.uint8)
    pmm = np.zeros((7, 128, F2), np.uint8)
    for i in range(7):
        Dp = 1 << i
        BX[i, pp ^ Dp, pp] = 1.0
        Sp = 2 << i
        base = pp & ~(Sp - 1)
        RS[i, base + Sp - 1 - (pp - base), pp] = 1.0
        pmx[i, :, :] = ((pp & Dp) != 0).astype(np.uint8)[:, None]
        pmm[i, :, :] = ((pp % Sp) >= Sp // 2).astype(np.uint8)[:, None]
    iota_s = np.arange(nn, dtype=np.float32).reshape(128, FREE)
    iota_m = (np.arange(2 * nn, dtype=np.float32) + 1).reshape(128, F2)
    UT = (np.arange(128)[:, None] < np.arange(128)[None, :]).astype(np.float32)
    idm = np.eye(128, dtype=np.float32)
    lbw = np.asarray(inputs["L1_attn_log_bandwidth"], np.float32)
    maps = []
    for c in range(NCORE):
        bhs = [((c * 4 + k) // H, (c * 4 + k) % H) for k in range(4)]
        maps.append({
            "zK": np.stack([zK[b, h] for b, h in bhs]),
            "zQ": np.stack([zQ[b, h] for b, h in bhs]),
            "vT": np.concatenate([np.ascontiguousarray(vC[b, h * 64:(h + 1) * 64, :].T) for b, h in bhs], 0),
            "lbw": np.array([[lbw[h]] for b, h in bhs], np.float32),
            "BX": BX, "RS": RS, "pmx": pmx, "pmm": pmm,
            "iota_s": iota_s, "iota_m": iota_m, "UT": UT,
            "LT": (np.arange(128)[:, None] <= np.arange(128)[None, :]).astype(np.float32),
            "e127": (np.arange(128) == 127).astype(np.float32).reshape(128, 1),
        })
    return maps


def kernel(**inputs):
    inputs = {k: np.asarray(v) for k, v in inputs.items()}
    cores = list(range(NCORE))
    resB = run_bass_kernel_spmd(build_B(), host_B(inputs), core_ids=cores)
    rb = resB.results
    r_t = np.concatenate([rb[c]["r_t"] for c in cores], 0).reshape(2, T, D)
    v_t = np.concatenate([rb[c]["v_t"] for c in cores], 0).reshape(2, T, D)
    asmT = lambda key: np.concatenate([rb[c][key] for c in cores], 1).reshape(D, 2, T).transpose(1, 0, 2)
    bres = {"r_t": r_t, "v_t": v_t, "rT": asmT("rT_o"), "kT": asmT("kT_o"), "wT": asmT("wT_o")}
    xnT = asmT("xnT_o")
    resC = run_bass_kernel_spmd(build_C(), host_C(inputs, bres), core_ids=cores)
    o = np.zeros((2, T, D), np.float32)
    for c in cores:
        for k in range(4):
            b, h = (c * 4 + k) // H, (c * 4 + k) % H
            o[b, :, h * 64:(h + 1) * 64] = resC.results[c]["o_t"][k]
    x = inputs["L0_attn_x"].astype(np.float32)
    resD = run_bass_kernel_spmd(build_D(), host_D(inputs, x, o, xnT), core_ids=cores)
    out0 = np.concatenate([resD.results[c]["out0_s"] for c in cores], 0).reshape(2, T, D)
    resE = run_bass_kernel_spmd(build_E(), host_E(inputs, out0), core_ids=cores)
    re_ = resE.results
    xn1 = np.concatenate([re_[c]["xn1_s"] for c in cores], 0).reshape(2, T, D)
    xn1T = np.concatenate([re_[c]["xn1T_s"] for c in cores], 1).reshape(D, 2, T).transpose(1, 0, 2)
    vC = np.concatenate([re_[c]["vC_s"] for c in cores], 1).reshape(D, 2, T).transpose(1, 0, 2)
    zq = np.concatenate([re_[c]["zq_s"] for c in cores], 1).reshape(16, 2, T).transpose(1, 0, 2)
    zk = np.concatenate([re_[c]["zk_s"] for c in cores], 1).reshape(16, 2, T).transpose(1, 0, 2)
    resF = run_bass_kernel_spmd(build_F(), host_F(inputs, zk, zq, vC), core_ids=cores)
    ctx = np.zeros((2, T, D), np.float32)
    for c in cores:
        for k in range(4):
            b, h = (c * 4 + k) // H, (c * 4 + k) % H
            ctx[b, :, h * 64:(h + 1) * 64] = resF.results[c]["ctxT"][k]
    resG = run_bass_kernel_spmd(build_G(), host_G(inputs, out0, ctx, xn1, xn1T), core_ids=cores)
    y = np.concatenate([resG.results[c]["y_s"] for c in cores], 0).reshape(2, T, D)
    return y
